# version 03 (final)
# baseline (speedup 1.0000x reference)
import numpy as np
import ml_dtypes
from contextlib import ExitStack

import concourse.bass as bass
import concourse.mybir as mybir
from concourse.bass_utils import run_bass_kernel_spmd

F32 = mybir.dt.float32
BF16 = mybir.dt.bfloat16
AF = mybir.ActivationFunctionType
ALU = mybir.AluOpType
AX = mybir.AxisListType

B, S, D = 4, 4096, 1024
H, HD = 16, 64
NKT = S // 128
TOWN = 2048
NT_OWN = TOWN // 128
NT_B = S // 128
NF = 2816
EPS = 1e-5

_MID = (0.25, 0.75, 1.25, 1.75, 2.5, 3.5, 5.0)
T2 = [t * t for t in _MID]

DBG = {"outs": (), "phase": 99}


class Ctr:
    def __init__(self, sem):
        self.sem = sem
        self.n = 0

    def inc(self, k=1):
        self.n += k
        return self.n


class CtrSet:
    def __init__(self, ctrs):
        self.ctrs = ctrs
        self.i = 0

    def use(self):
        c = self.ctrs[self.i]
        self.i = (self.i + 1) % len(self.ctrs)
        return c

    def snapshot(self):
        return [(c, c.n) for c in self.ctrs]


class SB:
    ENGS = ("sync", "gpsimd", "vector", "scalar", "tensor")

    def __init__(self):
        self.steps = {e: [] for e in self.ENGS}

    def w(self, eng, ctr, n=None):
        if isinstance(ctr, CtrSet):
            snap = ctr.snapshot() if n is None else n
            for c, nn_ in snap:
                self.w(eng, c, nn_)
            return
        if isinstance(ctr, list):
            for c, nn_ in ctr:
                self.w(eng, c, nn_)
            return
        if n is None:
            n = ctr.n
        if n <= 0:
            return
        self.steps[eng].append(("w", ctr.sem, n))

    def e(self, eng, fn, inc=None, dma=False):
        incs = []
        if inc is not None:
            if isinstance(inc, tuple):
                ctr, k = inc
            else:
                ctr, k = inc, (16 if dma else 1)
            if isinstance(ctr, CtrSet):
                ctr = ctr.use()
            if dma:
                self.w(eng, ctr, ctr.n)
            ctr.inc(k)
            incs.append((ctr.sem, k))
        self.steps[eng].append(("e", fn, incs))

    def mark(self, eng, ctr, k=1):
        if isinstance(ctr, CtrSet):
            ctr = ctr.use()
        for st in reversed(self.steps[eng]):
            if st[0] == "e":
                ctr.inc(k)
                st[2].append((ctr.sem, k))
                return
        raise RuntimeError("mark: no prior instruction on " + eng)

    def replay(self, eng_name, eng):
        for st in self.steps[eng_name]:
            if st[0] == "w":
                eng.wait_ge(st[1], st[2])
            else:
                inst = st[1](eng)
                for sem, k in st[2]:
                    inst.then_inc(sem, k)


def _emit_quant(sb, qs, C, c_dve):
    v = "vector"
    nb = C // 16
    x = qs["x"][:, 0:C]
    x3 = qs["x"][:, 0:C].rearrange("p (b k) -> p b k", k=16)
    u = qs["u"][:, 0:C]
    g = qs["g"][:, 0:C]
    h = qs["h"][:, 0:C]
    sg = qs["sg"][:, 0:C]
    q3 = qs["q"][:, 0:C].rearrange("p (b k) -> p b k", k=16)
    m = qs["m"][:, 0:nb]
    mc = qs["mc"][:, 0:nb]
    rm = qs["rm"][:, 0:nb]
    mcb = qs["mc"][:, 0:nb].unsqueeze(2).broadcast_to([128, nb, 16])
    rmb = qs["rm"][:, 0:nb].unsqueeze(2).broadcast_to([128, nb, 16])

    sb.e(v, lambda e: e.tensor_reduce(m.unsqueeze(2), x3, AX.X, ALU.max,
                                      apply_absolute_value=True), inc=c_dve)
    sb.w(v, c_dve)
    sb.e(v, lambda e: e.tensor_scalar(mc, m, 1.0 / 6.0, 1e-12, ALU.mult, ALU.max),
         inc=c_dve)
    sb.w(v, c_dve)
    sb.e(v, lambda e: e.reciprocal(rm, mc))
    sb.e(v, lambda e: e.tensor_scalar(sg, x, 0.0, 0.5, ALU.is_gt, ALU.subtract))
    sb.e(v, lambda e: e.tensor_tensor(u.rearrange("p (b k) -> p b k", k=16), x3, rmb,
                                      ALU.mult))
    sb.e(v, lambda e: e.tensor_tensor(x, u, u, ALU.mult))
    sq_ = x
    sb.e(v, lambda e: e.tensor_scalar(g, sq_, T2[0], None, ALU.is_gt))
    for t2 in (T2[1], T2[2], T2[3]):
        sb.e(v, lambda e, t2=t2: e.scalar_tensor_tensor(g, sq_, t2, g, ALU.is_gt, ALU.add))
    sb.e(v, lambda e: e.tensor_scalar(h, sq_, T2[6], 2.0, ALU.is_gt, ALU.mult))
    for t2 in (T2[4], T2[5]):
        sb.e(v, lambda e, t2=t2: e.scalar_tensor_tensor(h, sq_, t2, h, ALU.is_gt, ALU.add))
    sb.e(v, lambda e: e.scalar_tensor_tensor(u, h, 2.0, g, ALU.mult, ALU.add))
    sb.e(v, lambda e: e.tensor_tensor(g, u, sg, ALU.mult))
    sb.e(v, lambda e: e.tensor_tensor(q3, g.rearrange("p (b k) -> p b k", k=16), mcb,
                                      ALU.mult), inc=c_dve)


def _cls(j, kt):
    dk = (kt - j) % 32
    if dk in (31, 0, 1):
        return "band"
    if 2 <= dk <= 15 - j:
        return "R"
    if dk >= 32 - j:
        return "L"
    return "M"


def _udest(hh, region):
    b = 4 + 2 * hh
    if region == "band":
        return b, 0
    if region == "R":
        return b, 256
    if region == "L":
        return b + 1, 0
    return b + 1, 256


def build(dbg_outs=()):
    nc = bass.Bass("TRN2", target_bir_lowering=False,
                   detect_race_conditions=DBG.get("racedet", False))

    x_in = nc.dram_tensor("x", [S, D], F32, kind="ExternalInput").ap()
    wq_in = nc.dram_tensor("wq", [D, D], F32, kind="ExternalInput").ap()
    wk_in = nc.dram_tensor("wk", [D, D], F32, kind="ExternalInput").ap()
    wv_in = nc.dram_tensor("wv", [D, D], F32, kind="ExternalInput").ap()
    wp_in = nc.dram_tensor("wp", [D, D], F32, kind="ExternalInput").ap()
    pex_in = nc.dram_tensor("pex", [384, HD], F32, kind="ExternalInput").ap()
    wg_in = nc.dram_tensor("wg", [NF, D], F32, kind="ExternalInput").ap()
    wu_in = nc.dram_tensor("wu", [NF, D], F32, kind="ExternalInput").ap()
    wd_in = nc.dram_tensor("wd", [D, NF], F32, kind="ExternalInput").ap()
    emask_in = nc.dram_tensor("emask", [128, 8], F32, kind="ExternalInput").ap()
    y_out = nc.dram_tensor("y", [TOWN, D], F32, kind="ExternalOutput").ap()

    XN = nc.dram_tensor("XN", [S, D], BF16, kind="Internal").ap()
    XQ = nc.dram_tensor("XQ", [TOWN, D], BF16, kind="Internal").ap()
    WBQ = nc.dram_tensor("WBQ", [D, D], BF16, kind="Internal").ap()
    WBK = nc.dram_tensor("WBK", [D, D], BF16, kind="Internal").ap()
    WBV = nc.dram_tensor("WBV", [D, D], BF16, kind="Internal").ap()
    WBP = nc.dram_tensor("WBP", [D, D], BF16, kind="Internal").ap()
    WBG = nc.dram_tensor("WBG", [NF, D], BF16, kind="Internal").ap()
    WBU = nc.dram_tensor("WBU", [NF, D], BF16, kind="Internal").ap()
    WBD = nc.dram_tensor("WBD", [D, NF], BF16, kind="Internal").ap()
    PEX = nc.dram_tensor("PEX", [384, HD], BF16, kind="Internal").ap()
    WSK = nc.dram_tensor("WSK", [H * NT_OWN * 128, 512], BF16, kind="Internal").ap()
    ACAT = nc.dram_tensor("ACAT", [D, TOWN], BF16, kind="Internal").ap()
    HQD = nc.dram_tensor("HQD", [TOWN, NF], BF16, kind="Internal").ap()

    dbg = {}
    for name, shape, dt in dbg_outs:
        dbg[name] = nc.dram_tensor("dbg_" + name, shape, dt, kind="ExternalOutput").ap()

    sb = SB()
    PH = DBG.get("phase", 99)

    with ExitStack() as ex:
        def sbuf(name, shape, dt):
            return ex.enter_context(nc.sbuf_tensor(name, shape, dt))

        def sem(name):
            return Ctr(ex.enter_context(nc.semaphore(name)))

        xld = sbuf("xld", [128, 2, D], F32)
        lnst = sbuf("lnst", [128, 8], F32)
        xnb = sbuf("xnb", [128, 2, D], BF16)
        qs = {
            "x": sbuf("qs_x", [128, D], F32),
            "u": sbuf("qs_u", [128, D], F32),
            "g": sbuf("qs_g", [128, D], F32),
            "h": sbuf("qs_h", [128, D], F32),
            "sg": sbuf("qs_sg", [128, D], F32),
            "q": sbuf("qs_q", [128, D], BF16),
            "m": sbuf("qs_m", [128, 64], F32),
            "mc": sbuf("qs_mc", [128, 64], F32),
            "rm": sbuf("qs_rm", [128, 64], F32),
        }
        wcast = sbuf("wcast", [128, 2, D], BF16)
        pebuf = sbuf("pebuf", [128, 3 * HD], F32)
        pebc = sbuf("pebc", [128, 3 * HD], BF16)
        peT = sbuf("peT", [128, 384], BF16)

        kT = sbuf("kT", [128, 4, S], BF16)
        vv = sbuf("vv", [128, 8, NKT * 65], BF16)
        qT = sbuf("qT", [128, 4, TOWN], BF16)
        xnT = sbuf("xnT", [128, 2, 8, 512], BF16)
        wTs = sbuf("wTs", [128, 2, 24, 128], BF16)
        expT = sbuf("expT", [128, 2, 2048], BF16)
        expE = sbuf("expE", [128, 256], BF16)
        biasT = sbuf("biasT", [128, 12, 128], BF16)
        r2sb = sbuf("r2sb", [128, 2, 2, 384], BF16)
        wbc = sbuf("wbc", [128, 2, 2, 2, 128], BF16)
        clrm = sbuf("clrm", [1, 1536], F32)
        ssb = sbuf("ssb", [128, 2, 256], F32)
        usb = sbuf("usb", [128, 2, 1024], F32)
        yst = sbuf("yst", [128, 2, 256], BF16)
        ones65 = sbuf("ones65", [128, 65], BF16)
        ones32 = sbuf("ones32", [128, 65], F32)
        zro65 = sbuf("zro65", [128, 65], BF16)
        emsb = sbuf("emsb", [128, 8], F32)
        PS = ex.enter_context(nc.psum_tensor("PS", [128, 4096], F32))

        kTflat = kT[:].rearrange("p a b -> p (a b)")
        vvflat = vv[:].rearrange("p a b -> p (a b)")
        xqTf = xnT[:, 0, :, 0:128]
        wguT = kTflat[:, 0:16 * 352].rearrange("p (a b) -> p a b", b=352)
        hqT4 = kTflat[:, 0:88 * 128].rearrange("p (a b) -> p a b", b=128)
        wdT = kTflat[:, 11264:11264 + 22 * 128].rearrange("p (a b) -> p a b", b=128)
        wpTw = vvflat[:, 0:16 * 512].rearrange("p (a b) -> p a b", b=512)
        acatT = vvflat[:, 8192:8192 + 16 * 128].rearrange("p (a b) -> p a b", b=128)
        outac = qT[:].bitcast(F32).rearrange("p a b -> p (a b)") \
            .rearrange("p (a b) -> p a b", b=1024)
        thb = usb[:, 0, 0:352]

        def bank(b, lo=0, hi=512):
            return PS[:, 512 * b + lo:512 * b + hi]

        c_xld = sem("c_xld")
        c_ln = sem("c_ln")
        c_lns = sem("c_lns")
        c_lnx = sem("c_lnx")
        c_xst = sem("c_xst")
        c_wld = sem("c_wld")
        c_wst = sem("c_wst")
        c_qdv = sem("c_qdv")
        c_qst = sem("c_qst")
        c_pe = sem("c_pe")
        c_fwl = sem("c_fwl")
        c_fws = sem("c_fws")
        c_att = sem("c_att")
        c_vec = sem("c_vec")
        c_act2 = sem("c_act2")
        c_sq = CtrSet([sem("c_sq0"), sem("c_sq1")])
        c_sy2 = CtrSet([sem("c_sy0"), sem("c_sy1"), sem("c_sy2a"), sem("c_sy3")])
        c_gq = CtrSet([sem("c_gq0"), sem("c_gq1")])
        c_fin = sem("c_fin")

        vec_free = {}
        wsk_done = {}
        wb_ev = {}
        av_done = {}
        bias_free = {"n": 0}

        for a in range(3):
            sb.e("sync", lambda e, a=a: e.dma_start(out=pebuf[:, a * HD:(a + 1) * HD],
                                                    in_=pex_in[128 * a:128 * (a + 1), :]),
                 inc=c_pe, dma=True)
        sb.e("sync", lambda e: e.dma_start(out=emsb[:], in_=emask_in), inc=c_pe, dma=True)
        sb.w("vector", c_pe)
        sb.e("vector", lambda e: e.tensor_scalar(pebc[:], pebuf[:], 2.0, None, ALU.mult),
             inc=c_pe)
        sb.e("vector", lambda e: e.memset(ones65[:], 1.0))
        sb.e("vector", lambda e: e.memset(ones32[:], 1.0))
        sb.e("vector", lambda e: e.memset(zro65[:], 0.0))
        sb.w("sync", c_pe)
        for a in range(3):
            sb.e("sync", lambda e, a=a: e.dma_start(out=PEX[128 * a:128 * (a + 1), :],
                                                    in_=pebc[:, a * HD:(a + 1) * HD]),
                 inc=c_pe, dma=True)
        sb.w("sync", c_pe)

        def _petr(half):
            def f(e):
                with nc.allow_non_contiguous_dma(reason="tiny pe transpose"):
                    return e.dma_start_transpose(peT[64 * half:64 * (half + 1), :], PEX)
            return f
        sb.e("sync", _petr(0), inc=c_pe, dma=True)
        sb.e("sync", _petr(1), inc=c_pe, dma=True)

        for i in range(NT_B if PH >= 2 else 0):
            buf = i % 2
            if i >= 2:
                sb.w("gpsimd", c_xst, 16 * (i - 1))
            sb.e("gpsimd", lambda e, i=i, buf=buf: e.dma_start(
                out=xld[:, buf, :], in_=x_in[128 * i:128 * (i + 1), :]), inc=c_xld, dma=True)

        for i in range(NT_B if PH >= 2 else 0):
            buf = i % 2
            xt = xld[:, buf, :]
            mean = lnst[:, 0:1]; varp = lnst[:, 1:2]; rstd = lnst[:, 2:3]
            ssum = lnst[:, 3:4]
            sb.w("vector", c_xld, 16 * (i + 1))
            sb.e("vector", lambda e, xt=xt, s=ssum: e.reduce_sum(s, xt, AX.X), inc=c_ln)
            sb.w("vector", c_ln)
            sb.e("vector", lambda e, s=ssum, m=mean: e.tensor_scalar(
                m, s, 1.0 / D, None, ALU.mult), inc=c_ln)
            sb.w("vector", c_ln)
            d = qs["u"][:, 0:D]
            sb.e("vector", lambda e, xt=xt, m=mean, d=d: e.tensor_scalar(
                d, xt, m, None, ALU.subtract))
            d2 = qs["g"][:, 0:D]
            sb.e("vector", lambda e, d=d, d2=d2: e.tensor_tensor(d2, d, d, ALU.mult))
            sb.e("vector", lambda e, d2=d2, s=ssum: e.reduce_sum(s, d2, AX.X), inc=c_ln)
            sb.w("vector", c_ln)
            sb.e("vector", lambda e, s=ssum, vp=varp: e.tensor_scalar(
                vp, s, 1.0 / D, EPS, ALU.mult, ALU.add), inc=c_ln)
            sb.w("scalar", c_ln)
            sb.e("scalar", lambda e, vp=varp, r=rstd: e.activation(
                out=r, in_=vp, func=AF.Sqrt), inc=c_lns)
            sb.w("vector", c_lns)
            sb.e("vector", lambda e, r=rstd: e.reciprocal(r, r), inc=c_ln)
            sb.w("vector", c_ln)
            xout = xnb[:, buf, :]
            if i >= 2:
                sb.w("vector", c_xst, 16 * (i - 1))
            sb.e("vector", lambda e, d=d, r=rstd, xo=xout: e.tensor_scalar(
                xo, d, r, None, ALU.mult), inc=c_lnx)
            sb.w("sync", c_lnx)
            sb.e("sync", lambda e, i=i, xo=xout: e.dma_start(
                out=XN[128 * i:128 * (i + 1), :], in_=xo), inc=c_xst, dma=True)
            if i < NT_OWN:
                sb.e("vector", lambda e, d=d, r=rstd: e.tensor_scalar(
                    qs["x"][:, 0:D], d, r, None, ALU.mult))
                _emit_quant(sb, qs, D, c_qdv)
                sb.w("sync", c_qdv, c_qdv.n)
                sb.e("sync", lambda e, i=i: e.dma_start(
                    out=XQ[128 * i:128 * (i + 1), :], in_=qs["q"][:, 0:D]),
                    inc=c_qst, dma=True)
                sb.w("vector", c_qst, c_qst.n)

        for wi, (win, wout) in enumerate([(wq_in, WBQ), (wk_in, WBK),
                                          (wv_in, WBV), (wp_in, WBP)] if PH >= 3 else []):
            for t in range(8):
                j = wi * 8 + t
                buf = j % 2
                if j >= 2:
                    sb.w("gpsimd", c_wst, 16 * (j - 1))
                sb.e("gpsimd", lambda e, win=win, t=t, buf=buf: e.dma_start(
                    out=wcast[:, buf, :], in_=win[128 * t:128 * (t + 1), :]),
                    inc=c_wld, dma=True)
                sb.w("sync", c_wld, 16 * (j + 1))
                sb.e("sync", lambda e, wout=wout, t=t, buf=buf: e.dma_start(
                    out=wout[128 * t:128 * (t + 1), :], in_=wcast[:, buf, :]),
                    inc=c_wst, dma=True)

        def quant_job(load_src, store_dst, C):
            sb.w("sync", c_fws, c_fws.n)
            sb.w("vector", c_fws, c_fws.n)
            sb.e("sync", lambda e, src=load_src, C=C: e.dma_start(
                out=qs["x"][:, 0:C], in_=src), inc=c_fwl, dma=True)
            sb.w("vector", c_fwl, c_fwl.n)
            _emit_quant(sb, qs, C, c_qdv)
            sb.w("sync", c_qdv, c_qdv.n)
            sb.e("sync", lambda e, dst=store_dst, C=C: e.dma_start(
                out=dst, in_=qs["q"][:, 0:C]), inc=c_fws, dma=True)

        if PH >= 4 and DBG.get("ffnq", True):
            for win, wout in ((wg_in, WBG), (wu_in, WBU)):
                for t in range(NF // 128):
                    quant_job(win[128 * t:128 * (t + 1), :],
                              wout[128 * t:128 * (t + 1), :], D)
            for t in range(8):
                for cc in range(4):
                    quant_job(wd_in[128 * t:128 * (t + 1), 704 * cc:704 * (cc + 1)],
                              WBD[128 * t:128 * (t + 1), 704 * cc:704 * (cc + 1)], 704)

        def emit_qkv(ph):
            drow0 = 512 * ph
            pair_pe = {}
            for tw in range(8):
                twb = tw % 2
                own = tw < 4
                if tw >= 2:
                    sb.w("scalar", c_att, pair_pe.get((tw - 2, 3), 0))
                for k8 in range(8):
                    sb.e("scalar", lambda e, tw=tw, twb=twb, k8=k8: e.dma_start_transpose(
                        xnT[:, twb, k8, :],
                        XN[512 * tw:512 * (tw + 1), 128 * k8:128 * (k8 + 1)]),
                        inc=c_sq, dma=True)
                xn_ld = c_sq.snapshot()
                for p in range(4):
                    gidx = tw * 4 + p
                    pb = gidx % 2
                    if gidx >= 2:
                        gp = gidx - 2
                        sb.w("sync", c_att, pair_pe.get((gp // 4, gp % 4), 0))
                    projs = [(0, WBK), (1, WBV)] + ([(2, WBQ)] if own else [])
                    for wi, wsrc in projs:
                        for k8 in range(8):
                            sb.e("sync", lambda e, wsrc=wsrc, p=p, wi=wi, k8=k8, pb=pb:
                                 e.dma_start_transpose(
                                     wTs[:, pb, 8 * wi + k8, :],
                                     wsrc[drow0 + 128 * p:drow0 + 128 * (p + 1),
                                          128 * k8:128 * (k8 + 1)]),
                                 inc=c_sy2, dma=True)
                    w_ld = c_sy2.snapshot()
                    QS = DBG.get("qkv_sub", 4)
                    if QS < 2:
                        continue
                    sb.w("tensor", c_sq, xn_ld)
                    sb.w("tensor", c_sy2, w_ld)
                    sb.w("tensor", c_vec, vec_free.get(gidx - 1, 0))
                    for k8 in range(8):
                        sb.e("tensor", lambda e, k8=k8, pb=pb, twb=twb: e.matmul(
                            bank(twb), wTs[:, pb, 0 + k8, :], xnT[:, twb, k8, :],
                            start=(k8 == 0), stop=(k8 == 7)))
                    for mc in range(4 if QS >= 3 else 0):
                        for k8 in range(8):
                            sb.e("tensor", lambda e, k8=k8, mc=mc, pb=pb, twb=twb: e.matmul(
                                bank(2 + mc, 0, 128),
                                xnT[:, twb, k8, 128 * mc:128 * (mc + 1)],
                                wTs[:, pb, 8 + k8, :],
                                start=(k8 == 0), stop=(k8 == 7)))
                    if own and QS >= 4:
                        for k8 in range(8):
                            sb.e("tensor", lambda e, k8=k8, pb=pb, twb=twb: e.matmul(
                                bank(6 + twb), wTs[:, pb, 16 + k8, :], xnT[:, twb, k8, :],
                                start=(k8 == 0), stop=(k8 == 7)))
                    sb.mark("tensor", c_att)
                    pair_pe[(tw, p)] = c_att.n
                    sb.w("vector", c_att, c_att.n)
                    sb.e("vector", lambda e, p=p, tw=tw, twb=twb: e.tensor_copy(
                        kT[:, p, 512 * tw:512 * (tw + 1)], bank(twb)))
                    for mc in range(4 if QS >= 3 else 0):
                        gt = tw * 4 + mc
                        for hh in range(2):
                            sb.e("vector", lambda e, p=p, mc=mc, hh=hh, gt=gt: e.tensor_copy(
                                vv[:, 2 * p + hh, 65 * gt:65 * gt + 64],
                                bank(2 + mc, 64 * hh, 64 * (hh + 1))))
                    if own and QS >= 4:
                        sb.e("vector", lambda e, p=p, tw=tw, twb=twb: e.tensor_copy(
                            qT[:, p, 512 * tw:512 * (tw + 1)], bank(6 + twb)))
                    sb.mark("vector", c_vec)
                    vec_free[gidx] = c_vec.n
            for hh in range(8):
                sb.e("vector", lambda e, hh=hh: e.memset(
                    vv[:, hh, :].rearrange("p (a b) -> p a b", b=65)[:, :, 64:65], 1.0))
            sb.mark("vector", c_vec)

        def emit_wbuild(ph):
            wbn = DBG.get("wb_n", 64)
            for p in range(4):
                for qt in range(NT_OWN):
                    if p * NT_OWN + qt >= wbn:
                        continue
                    qtb = qt % 2
                    it = p * NT_OWN + qt
                    if it >= 2:
                        sb.w("tensor", c_sy2, wsk_done.get(it - 2, []))
                    sb.w("tensor", c_vec, wb_ev.get(it - 1, 0))
                    for hh in range(2):
                        sb.e("tensor", lambda e, p=p, qt=qt, hh=hh: e.matmul(
                            bank(hh, 0, 384),
                            qT[64 * hh:64 * (hh + 1), p, 128 * qt:128 * (qt + 1)],
                            peT[64 * hh:64 * (hh + 1), 0:384],
                            start=True, stop=True, tile_position=(64 * hh, 0)))
                    sb.mark("tensor", c_att)
                    WBS = DBG.get("wb_sub", 3)
                    sb.w("vector", c_att, c_att.n)
                    sb.w("vector", c_sy2, wsk_done.get(it - 1, []))
                    for hh in range(2):
                        sb.e("vector", lambda e, qtb=qtb, hh=hh: e.tensor_copy(
                            r2sb[:, qtb, hh, :], bank(hh, 0, 384)))
                        if WBS >= 2:
                            sb.e("vector", lambda e, qtb=qtb, hh=hh: e.tensor_copy(
                                wbc[:, qtb, hh, 0, :],
                                r2sb[:, qtb, hh, 128:129].broadcast_to([128, 128])))
                            sb.e("vector", lambda e, qtb=qtb, hh=hh: e.tensor_copy(
                                wbc[:, qtb, hh, 1, :],
                                r2sb[:, qtb, hh, 127:128].broadcast_to([128, 128])))
                    sb.mark("vector", c_vec)
                    wb_ev[it] = c_vec.n
                    sb.w("sync", c_vec, c_vec.n)
                    for hh in range(2 if WBS >= 3 else 0):
                        hg = 8 * ph + 2 * p + hh
                        base = (hg * NT_OWN + qt) * 128
                        sb.e("sync", lambda e, base=base, qtb=qtb, hh=hh: e.dma_start(
                            out=WSK[base:base + 128, 0:128], in_=wbc[:, qtb, hh, 0, :]),
                            inc=c_sy2, dma=True)
                        sb.e("sync", lambda e, base=base, qtb=qtb, hh=hh: e.dma_start(
                            out=WSK[base:base + 128, 128:255],
                            in_=r2sb[:, qtb, hh, 129:256]), inc=c_sy2, dma=True)
                        sb.e("sync", lambda e, base=base, qtb=qtb, hh=hh: e.dma_start(
                            out=WSK[base:base + 128, 255:383],
                            in_=r2sb[:, qtb, hh, 0:128]), inc=c_sy2, dma=True)
                        sb.e("sync", lambda e, base=base, qtb=qtb, hh=hh: e.dma_start(
                            out=WSK[base:base + 128, 383:511], in_=wbc[:, qtb, hh, 1, :]),
                            inc=c_sy2, dma=True)
                    wsk_done[it] = c_sy2.snapshot()

        def emit_attention(ph):
            attn = DBG.get("att_n", 32)
            for p in range(4):
                for qw in range(8):
                    if p * 8 + qw >= attn:
                        continue
                    j0, j1 = 2 * qw, 2 * qw + 1
                    ASUB = DBG.get("att_sub", 6)
                    ESUB = DBG.get("e_sub", 4)
                    is_edge = (qw == 0 or qw == 7) and ASUB >= 3
                    if is_edge:
                        ekt = 31 if qw == 0 else 16
                        ecol = 4 if qw == 0 else 5
                        jj = 0 if qw == 0 else 1
                        dregion = "R" if qw == 0 else "L"
                        for hh in range(2):
                            sb.e("tensor", lambda e, p=p, hh=hh, ekt=ekt, qw=qw, jj=jj:
                                 e.matmul(
                                     PS[:, 512 * hh:512 * hh + 128],
                                     kT[64 * hh:64 * (hh + 1), p,
                                        128 * ekt:128 * (ekt + 1)],
                                     qT[64 * hh:64 * (hh + 1), p,
                                        256 * qw + 128 * jj:256 * qw + 128 * (jj + 1)],
                                     start=True, stop=True, tile_position=(64 * hh, 0)))
                        sb.mark("tensor", c_att)
                        if ESUB >= 2:
                            sb.w("scalar", c_att, c_att.n)
                            sb.e("scalar", lambda e: e.activation(
                                out=expE[:].rearrange("p (a b) -> p a b", b=128),
                                in_=PS[:, 0:1024].rearrange("p (a b) -> p a b", b=512)
                                [:, :, 0:128],
                                func=AF.Exp, scale=0.125), inc=c_act2)
                        if ESUB >= 3:
                            sb.w("vector", c_act2, c_act2.n)
                            sb.e("vector", lambda e, ecol=ecol: e.tensor_scalar(
                                expE[:], expE[:], emsb[:, ecol:ecol + 1], None, ALU.mult),
                                inc=c_vec)
                        sb.w("tensor", c_act2, c_act2.n)
                    for hh in range(2):
                        for ri, pecol in enumerate((127, 128, 256)):
                            off = 1024 * hh + 256 * ri
                            sb.e("tensor", lambda e, p=p, hh=hh, pecol=pecol, off=off,
                                 qw=qw: e.matmul(
                                     PS[0:1, off:off + 256],
                                     peT[64 * hh:64 * (hh + 1), pecol:pecol + 1],
                                     qT[64 * hh:64 * (hh + 1), p,
                                        256 * qw:256 * (qw + 1)],
                                     start=True, stop=True, tile_position=(64 * hh, 0)))
                    sb.mark("tensor", c_att)
                    sb.w("scalar", c_att, c_att.n)
                    sb.e("scalar", lambda e: e.activation(
                        out=clrm[0:1, 0:768], in_=PS[0:1, 0:768], func=AF.Exp,
                        scale=0.125))
                    sb.e("scalar", lambda e: e.activation(
                        out=clrm[0:1, 768:1536], in_=PS[0:1, 1024:1792], func=AF.Exp,
                        scale=0.125), inc=c_act2)
                    clrm_done = c_act2.n
                    for b in range(4, 8 if ASUB >= 2 else 4):
                        sb.e("tensor", lambda e, b=b: e.matmul(
                            PS[0:65, 512 * b:512 * (b + 1)], zro65[:], kT[:, 0, 0:512],
                            start=True, stop=False, skip_group_check=True))
                    sb.mark("tensor", c_att)

                    if is_edge and ESUB >= 4:
                        sb.w("tensor", c_vec, c_vec.n)
                        for hh in range(2):
                            ubank, ulo = _udest(hh, dregion)
                            sb.e("tensor", lambda e, p=p, hh=hh, ekt=ekt, jj=jj,
                                 ubank=ubank, ulo=ulo: e.matmul(
                                     PS[0:65, 512 * ubank + ulo + 128 * jj:
                                        512 * ubank + ulo + 128 * (jj + 1)],
                                     vv[:, 2 * p + hh, 65 * ekt:65 * (ekt + 1)],
                                     expE[:, 128 * hh:128 * (hh + 1)],
                                     start=False, stop=False, skip_group_check=True))
                        sb.mark("tensor", c_att)

                    sb.w("scalar", c_vec, bias_free["n"])
                    for jj, j in (((0, j0), (1, j1)) if ASUB >= 4 else ()):
                        for sl in range(3):
                            for hh in range(2):
                                hg = 8 * ph + 2 * p + hh
                                base = (hg * NT_OWN + j) * 128
                                slot = 6 * jj + 2 * sl + hh
                                sb.e("scalar", lambda e, base=base, sl=sl, slot=slot:
                                     e.dma_start_transpose(
                                         biasT[:, slot, :],
                                         bass.AP(tensor=WSK.tensor,
                                                 offset=base * 512 + 127 + 128 * sl,
                                                 ap=[[511, 128], [1, 128]])),
                                     inc=c_sq, dma=True)
                    bias_ld = c_sq.snapshot()
                    bias_rdy = c_vec.n
                    if (qw == 0 or qw == 7) and ASUB >= 4:
                        jj = 0 if qw == 0 else 1
                        sl = 0 if qw == 0 else 2
                        mcol = 0 if qw == 0 else 2
                        sb.w("vector", c_sq, bias_ld)
                        for hh in range(2):
                            slot = 6 * jj + 2 * sl + hh
                            sb.e("vector", lambda e, slot=slot, mcol=mcol: e.tensor_scalar(
                                biasT[:, slot, :], biasT[:, slot, :],
                                emsb[:, mcol:mcol + 1], emsb[:, mcol + 1:mcol + 2],
                                ALU.mult, ALU.add), inc=c_vec)
                        bias_rdy = c_vec.n

                    av_done.clear()
                    prev_exp = c_act2.n
                    for g in range(8 if ASUB >= 5 else 0):
                        sb.w("tensor", c_act2, prev_exp)
                        for i in range(4):
                            kt = 4 * g + i
                            for hh in range(2):
                                sub = i + 4 * hh
                                sb.e("tensor", lambda e, p=p, hh=hh, kt=kt, sub=sub,
                                     qw=qw: e.matmul(
                                         PS[:, 256 * sub:256 * (sub + 1)],
                                         kT[64 * hh:64 * (hh + 1), p,
                                            128 * kt:128 * (kt + 1)],
                                         qT[64 * hh:64 * (hh + 1), p,
                                            256 * qw:256 * (qw + 1)],
                                         start=True, stop=True,
                                         tile_position=(64 * hh, 0)))
                        sb.mark("tensor", c_att)
                        qk_done = c_att.n
                        sb.w("vector", c_att, qk_done)
                        sb.w("vector", c_sq, bias_ld)
                        sb.w("vector", c_vec, bias_rdy)
                        nadds = 0
                        for i in range(4):
                            kt = 4 * g + i
                            for jj, j in (((0, j0), (1, j1)) if ASUB >= 4 else ()):
                                dk = (kt - j) % 32
                                if dk not in (31, 0, 1):
                                    continue
                                sl = {31: 0, 0: 1, 1: 2}[dk]
                                for hh in range(2):
                                    sub = i + 4 * hh
                                    slot = 6 * jj + 2 * sl + hh
                                    sb.e("vector", lambda e, sub=sub, jj=jj, slot=slot:
                                         e.tensor_tensor(
                                             PS[:, 256 * sub + 128 * jj:
                                                256 * sub + 128 * (jj + 1)],
                                             PS[:, 256 * sub + 128 * jj:
                                                256 * sub + 128 * (jj + 1)],
                                             biasT[:, slot, :], ALU.add))
                                    nadds += 1
                        if nadds:
                            sb.mark("vector", c_vec)
                        badd_done = c_vec.n
                        sb.w("scalar", c_att, qk_done)
                        sb.w("scalar", c_vec, badd_done)
                        sb.w("scalar", c_att, av_done.get(g - 2, 0))
                        sb.e("scalar", lambda e, g=g: e.activation(
                            out=expT[:, g % 2, :], in_=PS[:, 0:2048], func=AF.Exp,
                            scale=0.125), inc=c_act2)
                        prev_exp = c_act2.n
                        sb.w("tensor", c_act2, prev_exp)
                        for i in range(4):
                            kt = 4 * g + i
                            for hh in range(2):
                                sub = i + 4 * hh
                                r0 = _cls(j0, kt)
                                r1 = _cls(j1, kt)
                                spans = ([(0, 256, r0)] if r0 == r1 else
                                         [(0, 128, r0), (128, 256, r1)])
                                for lo, hi, rr in spans:
                                    ubank, ulo = _udest(hh, rr)
                                    sb.e("tensor", lambda e, p=p, hh=hh, kt=kt, sub=sub,
                                         lo=lo, hi=hi, ubank=ubank, ulo=ulo, g=g:
                                         e.matmul(
                                             PS[0:65, 512 * ubank + ulo + lo:
                                                512 * ubank + ulo + hi],
                                             vv[:, 2 * p + hh, 65 * kt:65 * (kt + 1)],
                                             expT[:, g % 2,
                                                  256 * sub + lo:256 * sub + hi],
                                             start=False, stop=False,
                                             skip_group_check=True))
                        sb.mark("tensor", c_att)
                        av_done[g] = c_att.n
                    bias_free["n"] = c_vec.n

                    if ASUB < 6:
                        continue
                    all_av = c_att.n
                    sb.w("tensor", c_act2, prev_exp)
                    for hh in range(2):
                        for ri in range(3):
                            idx = 3 * hh + ri
                            sb.e("tensor", lambda e, idx=idx, hh=hh, ri=ri: e.matmul(
                                PS[0:65, 256 * idx:256 * (idx + 1)],
                                ones32[0:1, :],
                                clrm[0:1, 768 * hh + 256 * ri:768 * hh + 256 * (ri + 1)],
                                start=True, stop=True))
                    sb.mark("tensor", c_att)
                    brd_done = c_att.n
                    for hh in range(2):
                        sb.w("vector", c_att, all_av)
                        sb.e("vector", lambda e, hh=hh: e.tensor_copy(
                            usb[0:65, hh, 0:512],
                            PS[0:65, 512 * (4 + 2 * hh):512 * (4 + 2 * hh) + 512]))
                        sb.e("vector", lambda e, hh=hh: e.tensor_copy(
                            usb[0:65, hh, 512:1024],
                            PS[0:65, 512 * (5 + 2 * hh):512 * (5 + 2 * hh) + 512]))
                        sb.w("vector", c_att, brd_done)
                        sb.e("vector", lambda e, hh=hh: e.tensor_tensor(
                            ssb[0:65, hh, :], usb[0:65, hh, 256:512],
                            PS[0:65, 256 * (3 * hh + 0):256 * (3 * hh + 1)], ALU.mult))
                        sb.e("vector", lambda e, hh=hh: e.tensor_tensor(
                            ssb[0:65, hh, :], ssb[0:65, hh, :], usb[0:65, hh, 0:256],
                            ALU.add))
                        sb.e("vector", lambda e, hh=hh: e.tensor_tensor(
                            usb[0:65, hh, 0:256], usb[0:65, hh, 512:768],
                            PS[0:65, 256 * (3 * hh + 1):256 * (3 * hh + 2)], ALU.mult))
                        sb.e("vector", lambda e, hh=hh: e.tensor_tensor(
                            ssb[0:65, hh, :], ssb[0:65, hh, :], usb[0:65, hh, 0:256],
                            ALU.add))
                        sb.e("vector", lambda e, hh=hh: e.tensor_tensor(
                            usb[0:65, hh, 0:256], usb[0:65, hh, 768:1024],
                            PS[0:65, 256 * (3 * hh + 2):256 * (3 * hh + 3)], ALU.mult))
                        sb.e("vector", lambda e, hh=hh: e.tensor_tensor(
                            ssb[0:65, hh, :], ssb[0:65, hh, :], usb[0:65, hh, 0:256],
                            ALU.add), inc=c_vec)
                        sb.w("vector", c_vec)
                        sb.e("vector", lambda e, hh=hh: e.reciprocal(
                            ssb[64:65, hh, :], ssb[64:65, hh, :]), inc=c_vec)
                        sb.w("tensor", c_vec, c_vec.n)
                        rb = 1536 if hh == 0 else 2048
                        sb.e("tensor", lambda e, hh=hh, rb=rb: e.matmul(
                            PS[0:65, rb:rb + 256],
                            ones32[64:65, :], ssb[64:65, hh, :],
                            start=True, stop=True, tile_position=(64, 0)), inc=c_att)
                        sb.w("vector", c_att, c_att.n)
                        sb.e("vector", lambda e, hh=hh, rb=rb: e.tensor_tensor(
                            yst[0:64, hh, :], ssb[0:64, hh, :],
                            PS[0:64, rb:rb + 256], ALU.mult),
                            inc=c_vec)
                        sb.w("gpsimd", c_vec, c_vec.n)
                        hg = 8 * ph + 2 * p + hh
                        sb.e("gpsimd", lambda e, hg=hg, qw=qw, hh=hh: e.dma_start(
                            out=ACAT[64 * hg:64 * (hg + 1), 256 * qw:256 * (qw + 1)],
                            in_=yst[0:64, hh, :]), inc=c_gq, dma=True)
                    sb.w("vector", c_gq)
                    sb.w("tensor", c_vec, c_vec.n)
                    sb.w("tensor", c_gq)

        def emit_ffn():
            for nw in range(8):
                sb.w("sync", c_att, ffn_pe.get("prev", 0))
                for wi, wsrc in ((0, WBG), (1, WBU)):
                    for k8 in range(8):
                        sb.e("sync", lambda e, wsrc=wsrc, wi=wi, k8=k8, nw=nw:
                             e.dma_start_transpose(
                                 wguT[:, 8 * wi + k8, :],
                                 wsrc[352 * nw:352 * (nw + 1),
                                      128 * k8:128 * (k8 + 1)]),
                             inc=c_sy2, dma=True)
                wg_ld = c_sy2.snapshot()
                for tt in range(NT_OWN):
                    sb.w("scalar", c_att, ffn_pe.get("prev", 0))
                    for k8 in range(8):
                        sb.e("scalar", lambda e, tt=tt, k8=k8: e.dma_start_transpose(
                            xqTf[:, k8, :],
                            XQ[128 * tt:128 * (tt + 1), 128 * k8:128 * (k8 + 1)]),
                            inc=c_sq, dma=True)
                    xq_ld = c_sq.snapshot()
                    sb.w("tensor", c_sy2, wg_ld)
                    sb.w("tensor", c_sq, xq_ld)
                    sb.w("tensor", c_vec, c_vec.n)
                    for wi in range(2):
                        for k8 in range(8):
                            sb.e("tensor", lambda e, wi=wi, k8=k8: e.matmul(
                                bank(wi, 0, 352), xqTf[:, k8, :], wguT[:, 8 * wi + k8, :],
                                start=(k8 == 0), stop=(k8 == 7)))
                    sb.mark("tensor", c_att)
                    ffn_pe["prev"] = c_att.n
                    sb.w("scalar", c_att, c_att.n)
                    sb.e("scalar", lambda e: e.activation(
                        out=thb, in_=bank(0, 0, 352), func=AF.Tanh, scale=0.5),
                        inc=c_act2)
                    sb.w("vector", c_act2, c_act2.n)
                    sb.e("vector", lambda e: e.scalar_tensor_tensor(
                        qs["u"][:, 0:352], thb, 1.0, bank(0, 0, 352), ALU.add, ALU.mult))
                    sb.e("vector", lambda e: e.tensor_scalar(
                        qs["u"][:, 0:352], qs["u"][:, 0:352], 0.5, None, ALU.mult))
                    sb.e("vector", lambda e: e.tensor_tensor(
                        qs["x"][:, 0:352], qs["u"][:, 0:352], bank(1, 0, 352), ALU.mult),
                        inc=c_vec)
                    _emit_quant(sb, qs, 352, c_qdv)
                    sb.w("sync", c_qdv, c_qdv.n)
                    sb.e("sync", lambda e, tt=tt, nw=nw: e.dma_start(
                        out=HQD[128 * tt:128 * (tt + 1), 352 * nw:352 * (nw + 1)],
                        in_=qs["q"][:, 0:352]), inc=c_sy2, dma=True)
                    sb.w("vector", c_sy2)

        def emit_epilogue():
            for dpw in range(2):
                for k8 in range(8):
                    sb.e("sync", lambda e, dpw=dpw, k8=k8: e.dma_start_transpose(
                        wpTw[:, 8 * dpw + k8, :],
                        WBP[512 * dpw:512 * (dpw + 1), 128 * k8:128 * (k8 + 1)]),
                        inc=c_sy2, dma=True)
            wp_ld = c_sy2.snapshot()
            for tb in range(4):
                tts = [4 * tb + i for i in range(4)]
                sb.w("sync", c_att, dwn_pe.get(tb - 1, 0))
                for ii, tt in enumerate(tts):
                    for nt in range(22):
                        sb.e("sync", lambda e, tt=tt, ii=ii, nt=nt: e.dma_start_transpose(
                            hqT4[:, 22 * ii + nt, :],
                            HQD[128 * tt:128 * (tt + 1), 128 * nt:128 * (nt + 1)]),
                            inc=c_sy2, dma=True)
                hq_ld = c_sy2.snapshot()
                for ii, tt in enumerate(tts):
                    ttb = ii % 2
                    sb.w("gpsimd", c_vec, wp_ev.get(tt - 2, 0))
                    sb.e("gpsimd", lambda e, tt=tt, ttb=ttb: e.dma_start(
                        out=xld[:, ttb, :], in_=x_in[128 * tt:128 * (tt + 1), :]),
                        inc=c_gq, dma=True)
                    sb.w("sync", c_att, wp_pe.get(tt - 2, 0))
                    for k8 in range(8):
                        sb.e("sync", lambda e, tt=tt, ttb=ttb, k8=k8: e.dma_start(
                            out=acatT[:, 8 * ttb + k8, :],
                            in_=ACAT[128 * k8:128 * (k8 + 1),
                                     128 * tt:128 * (tt + 1)]),
                            inc=c_sy2, dma=True)
                    ac_ld = c_sy2.snapshot()
                    sb.w("tensor", ac_ld)
                    sb.w("tensor", wp_ld)
                    sb.w("tensor", c_vec, wp_ev.get(tt - 1, 0))
                    for dpw in range(2):
                        for k8 in range(8):
                            sb.e("tensor", lambda e, dpw=dpw, k8=k8, ttb=ttb: e.matmul(
                                bank(6 + dpw), acatT[:, 8 * ttb + k8, :],
                                wpTw[:, 8 * dpw + k8, :],
                                start=(k8 == 0), stop=(k8 == 7)))
                    sb.mark("tensor", c_att)
                    wp_pe[tt] = c_att.n
                    sb.w("vector", c_att, c_att.n)
                    sb.w("vector", c_gq)
                    sb.w("vector", c_fin, epi_done.get(tt - 2, 0))
                    for dpw in range(2):
                        sb.e("vector", lambda e, dpw=dpw, ii=ii, ttb=ttb: e.tensor_tensor(
                            outac[:, ii, 512 * dpw:512 * (dpw + 1)], bank(6 + dpw),
                            xld[:, ttb, 512 * dpw:512 * (dpw + 1)], ALU.add))
                    sb.mark("vector", c_vec)
                    wp_ev[tt] = c_vec.n
                for dw in range(8):
                    sb.w("sync", c_att, dwn_pe.get((tb, dw - 1), 0))
                    for nt in range(22):
                        sb.e("sync", lambda e, dw=dw, nt=nt: e.dma_start_transpose(
                            wdT[:, nt, :],
                            WBD[128 * dw:128 * (dw + 1), 128 * nt:128 * (nt + 1)]),
                            inc=c_sy2, dma=True)
                    wd_ld = c_sy2.snapshot()
                    sb.w("tensor", wd_ld)
                    sb.w("tensor", hq_ld)
                    sb.w("tensor", c_vec, dwn_ev.get((tb, dw - 1), 0))
                    for ii in range(4):
                        for nt in range(22):
                            sb.e("tensor", lambda e, ii=ii, nt=nt: e.matmul(
                                bank(ii, 0, 128), hqT4[:, 22 * ii + nt, :],
                                wdT[:, nt, :],
                                start=(nt == 0), stop=(nt == 21)))
                    sb.mark("tensor", c_att)
                    dwn_pe[(tb, dw)] = c_att.n
                    sb.w("vector", c_att, c_att.n)
                    sb.w("vector", c_vec, wp_ev[tts[-1]])
                    for ii in range(4):
                        sb.e("vector", lambda e, ii=ii, dw=dw: e.tensor_tensor(
                            outac[:, ii, 128 * dw:128 * (dw + 1)],
                            outac[:, ii, 128 * dw:128 * (dw + 1)],
                            bank(ii, 0, 128), ALU.add))
                    sb.mark("vector", c_vec)
                    dwn_ev[(tb, dw)] = c_vec.n
                dwn_pe[tb] = c_att.n
                sb.w("gpsimd", c_vec, c_vec.n)
                for ii, tt in enumerate(tts):
                    sb.e("gpsimd", lambda e, tt=tt, ii=ii: e.dma_start(
                        out=y_out[128 * tt:128 * (tt + 1), :], in_=outac[:, ii, :]),
                        inc=c_fin, dma=True)
                    epi_done[tt] = c_fin.n

        ffn_pe = {}
        dwn_pe = {}
        dwn_ev = {}
        epi_done = {}
        wp_pe = {}
        wp_ev = {}

        if PH >= 5:
            sb.w("tensor", c_xst, 16 * NT_B)
            sb.w("tensor", c_wst, 16 * 24)
            sb.w("scalar", c_xst, 16 * NT_B)
            sb.w("sync", c_wst, 16 * 24)
            sb.w("tensor", c_pe, c_pe.n)
            ATT = DBG.get("att", 3)
            for ph in range(2 if ATT >= 3 else 1):
                bias_free["n"] = 0
                emit_qkv(ph)
                sb.w("tensor", c_vec, c_vec.n)
                if ATT >= 2:
                    emit_wbuild(ph)
                    sb.w("tensor", c_sy2)
                    sb.w("scalar", c_sy2)
                if ATT >= 3:
                    emit_attention(ph)

        if PH >= 6:
            sb.w("tensor", c_qst, c_qst.n)
            sb.w("tensor", c_fws, 16 * 44)
            sb.w("sync", c_fws, 16 * 44)
            sb.w("sync", c_qst, c_qst.n)
            sb.w("sync", c_gq)
            sb.w("scalar", c_qst, c_qst.n)
            sb.w("vector", c_att, c_att.n)
            emit_ffn()

        if PH >= 7:
            sb.w("tensor", c_fws, c_fws.n)
            sb.w("sync", c_fws, c_fws.n)
            sb.w("sync", c_sy2)
            sb.w("tensor", c_sy2)
            emit_epilogue()
            sb.w("gpsimd", c_fin, c_fin.n)

        dbg_srcs = {"xn": (XN, c_xst), "xq": (XQ, c_qst), "wgq": (WBG, c_fws),
                    "wdq": (WBD, c_fws), "pet": (peT[:], c_pe),
                    "acat": (ACAT, c_gq), "hqd": (HQD, c_sy2),
                    "kt": (kT[:], c_vec), "qt": (qT[:], c_vec)}
        c_dbg = sem("c_dbg")
        for nm in dbg:
            if nm in dbg_srcs:
                src_, ctr_ = dbg_srcs[nm]
                sb.w("gpsimd", ctr_)
                sb.e("gpsimd", lambda e, s=src_, d=dbg[nm]: e.dma_start(out=d, in_=s),
                     inc=c_dbg, dma=True)
        sb.w("gpsimd", c_dbg, c_dbg.n)

        if PH < 7:
            sb.w("gpsimd", c_qst, c_qst.n)
            sb.w("gpsimd", c_gq)
            sb.w("gpsimd", c_sy2)
            sb.w("gpsimd", c_vec, c_vec.n)
            sb.w("gpsimd", c_att, c_att.n)
            sb.w("gpsimd", c_act2, c_act2.n)
            sb.w("gpsimd", c_sq)
            sb.e("gpsimd", lambda e: e.dma_start(out=y_out[0:128, :], in_=qs["x"][:, 0:D]),
                 inc=c_fin, dma=True)
            sb.w("gpsimd", c_fin, c_fin.n)

        with nc.Block() as block:
            @block.sync
            def _(e):
                sb.replay("sync", e)

            @block.gpsimd
            def _(e):
                sb.replay("gpsimd", e)

            @block.vector
            def _(e):
                sb.replay("vector", e)

            @block.scalar
            def _(e):
                sb.replay("scalar", e)

            @block.tensor
            def _(e):
                sb.replay("tensor", e)

    return nc


def _shard_inputs(inputs):
    x = np.ascontiguousarray(np.asarray(inputs["L0_attn_x"]))
    pe = np.ascontiguousarray(np.asarray(inputs["L0_attn_pos_emb_k"]))
    wq = np.ascontiguousarray(np.asarray(inputs["L0_attn_Wq"], np.float32))
    wk = np.ascontiguousarray(np.asarray(inputs["L0_attn_Wk"], np.float32))
    wv = np.ascontiguousarray(np.asarray(inputs["L0_attn_Wv"], np.float32))
    wp = np.ascontiguousarray(np.asarray(inputs["L0_attn_Wp"], np.float32))
    wg = np.ascontiguousarray(np.asarray(inputs["L0_ffn_w_gate"], np.float32))
    wu = np.ascontiguousarray(np.asarray(inputs["L0_ffn_w_up"], np.float32))
    wd = np.ascontiguousarray(np.asarray(inputs["L0_ffn_w_down"], np.float32))
    maps = []
    for c in range(8):
        b, half = c // 2, c % 2
        off = TOWN * half
        xr = np.roll(x[b], -off, axis=0)
        pex = np.zeros((384, HD), np.float32)
        pex[0:256] = pe
        pex[256] = pe[127] if half == 0 else pe[128]
        em = np.zeros((128, 8), np.float32)
        k0 = 1.0 if half == 0 else 0.0
        k1 = 1.0 - k0
        em[:, 0] = 1.0 - k0
        em[:, 1] = -k0 * 1e9
        em[:, 2] = 1.0 - k1
        em[:, 3] = -k1 * 1e9
        em[:, 4] = k0
        em[:, 5] = k1
        maps.append({
            "x": np.ascontiguousarray(xr.astype(np.float32)),
            "wq": wq, "wk": wk, "wv": wv, "wp": wp,
            "pex": pex, "wg": wg, "wu": wu, "wd": wd,
            "emask": em,
        })
    return maps


def kernel(**inputs):
    maps = _shard_inputs(inputs)
    nc = build(dbg_outs=DBG["outs"])
    res = run_bass_kernel_spmd(nc, maps, core_ids=list(range(8)))
    out = np.zeros((B, S, D), np.float32)
    for c in range(8):
        b, half = c // 2, c % 2
        off = TOWN * half
        out[b, off:off + TOWN, :] = res.results[c]["y"]
    if DBG["outs"]:
        return out, res
    return out
